# version 06 (final)
# baseline (speedup 1.0000x reference)
import numpy as np
from contextlib import ExitStack

import concourse.bass as bass
import concourse.mybir as mybir
from concourse.bass_utils import run_bass_kernel_spmd

F32 = mybir.dt.float32
BF16 = mybir.dt.bfloat16
AF = mybir.ActivationFunctionType
ALU = mybir.AluOpType
AX = mybir.AxisListType

D = 1024
T = 2048
NDT = 8
NTT = 16
HEADS = 4
HDV = 512
NCH = 16
NSA = 4
NSF = 8
NEXP = 10
LNSCALE = float(np.log(np.float32(0.0625)))
LN16 = float(np.log(np.float32(16.0)))
EPS = 1e-5

PHASES = "full"
DEBUG_SCRATCH_OUT = False


class P:
    ENG = ("t", "v", "s", "g", "y")

    def __init__(self, nc, stack):
        self.nc = nc
        self.q = {e: [] for e in self.ENG}
        names = ["T", "V", "S", "G", "DX", "DW", "DE0", "DE1", "DSP",
                 "DSI0", "DSI1", "DSI2", "DCO", "DOUT", "BOOT", "FIN"]
        self.sem = {}
        self.cnt = {}
        for n in names:
            self.sem[n] = stack.enter_context(nc.semaphore(n))
            self.cnt[n] = 0

    def wait(self, e, cname, val):
        if val is None or val <= 0:
            return
        self.q[e].append(lambda eng, h=self.sem[cname], v=val: eng.wait_ge(h, v))

    def op(self, e, emit, inc=None, n=1):
        if inc is None:
            self.q[e].append(lambda eng, f=emit: f(eng))
            return None
        self.cnt[inc] += n
        self.q[e].append(lambda eng, f=emit, h=self.sem[inc], n=n: f(eng).then_inc(h, n))
        return self.cnt[inc]

    def barrier(self):
        snap = dict(self.cnt)
        for e in self.ENG:
            for n in ("T", "V", "S", "G"):
                self.wait(e, n, snap[n])
        return snap

    def emit_preamble(self):
        nc = self.nc
        ks = nc._kernel_sem_range
        nres = 2 + (1 if nc._bir_kernel_barrier_sem is not None else 0) \
            + len(nc._monotonic_sems)
        rng = range(ks.start + nres, ks.stop)
        self.q["g"].append(lambda g, r=rng: g.dma_reset(r))
        self.cnt["BOOT"] += 1
        self.q["g"].append(lambda g, r=rng, h=self.sem["BOOT"]:
                           g.sem_clear(r).then_inc(h, 1))
        for e in ("t", "v", "s", "y"):
            self.wait(e, "BOOT", 1)

    def emit_postamble(self):
        return

    def emit(self, block):
        secs = {"t": block.tensor, "v": block.vector, "s": block.scalar,
                "g": block.gpsimd, "y": block.sync}
        for e in self.ENG:
            lst = self.q[e]

            def body(eng, lst=lst):
                for fn in lst:
                    fn(eng)
            secs[e](body)


def build_nc(phases=None, debug_scratch=None):
    phases = PHASES if phases is None else phases
    debug_scratch = DEBUG_SCRATCH_OUT if debug_scratch is None else debug_scratch
    nc = bass.Bass("TRN2", target_bir_lowering=False)
    dt = nc.dram_tensor
    io = {}

    def din(name, shape):
        io[name] = dt(name, shape, F32, kind="ExternalInput").ap()

    def dinb(name, shape):
        io[name] = dt(name, shape, BF16, kind="ExternalInput").ap()

    din("xT", [D, T])
    din("wq", [D, D]); din("wk", [D, D]); din("wv", [D, 2048]); din("wg", [D, 2048])
    din("wgk1", [D, 16]); din("wgk2e", [17, D]); din("wo", [2048, D])
    din("rmsw4", [128, 4]); din("gatew", [D, 8]); din("gbias", [128, 8])
    din("fwg", [8, D, 512]); din("fwu", [8, D, 512]); din("fwd", [8, 512, D])
    din("swg", [D, D]); din("swu", [D, D]); din("swd", [D, D])
    dinb("ones_col_bf", [128, 1])
    din("ones_row_f32", [1, 128])
    dinb("ones_row_bf", [1, 128])
    dinb("ones_row512_bf", [1, 512])
    dinb("eye_bf", [128, 128])
    din("eye_f32", [128, 128])
    dinb("triu4_bf", [128, 512])
    din("iota8", [128, 128]); din("iotaB", [128, 128]); din("meta", [128, 4])
    din("cbias", [128, 4])
    io["outT"] = dt("outT", [D, T], F32, kind="ExternalOutput").ap()
    sk = "ExternalOutput" if debug_scratch else "Internal"
    io["xsp"] = dt("xsp", [128, NDT * T], F32, kind=sk).ap()
    io["olb"] = dt("olb", [128, HEADS * 8192], BF16, kind=sk).ap()
    io["qtb"] = dt("qtb", [128, HEADS * 4096], BF16, kind=sk).ap()
    io["cin"] = dt("cin", [1, 128 * 4104], F32, kind=sk).ap()
    io["cout"] = dt("cout", [4, 128 * 4104], F32, kind="Internal").ap()

    with ExitStack() as top:
        build_body(nc, io, top, phases)
    return nc


def build_body(nc, io, top, phases):
    p = P(nc, top)

    _names = {}

    def sb(name, shape, dtp, stack=top):
        k = _names.get(name, 0)
        _names[name] = k + 1
        if k:
            name = f"{name}_u{k}"
        return stack.enter_context(nc.sbuf_tensor(name, shape, dtp))

    def pt(stack, name, shape, dtp=F32):
        k = _names.get(name, 0)
        _names[name] = k + 1
        if k:
            name = f"{name}_u{k}"
        return stack.enter_context(nc.psum_tensor(name, shape, dtp))

    gapb = sb("gapb", [128, 32], F32)

    def vgap():
        p.op("v", lambda v: v.memset(gapb[:], 0.0), None)

    xn = sb("xn", [128, NDT * T], BF16)
    gw_sb = sb("gw_sb", [128, 64], BF16)
    gbias_sb = sb("gbias_sb", [128, 8], F32)
    rmsw4_sb = sb("rmsw4_sb", [128, 4], F32)
    meta_sb = sb("meta_sb", [128, 4], F32)
    onescol = sb("onescol", [128, 1], BF16)
    onesrow = sb("onesrow", [1, 128], F32)
    onesrowb = sb("onesrowb", [1, 128], BF16)
    eye_bf = sb("eye_bf_s", [128, 128], BF16)
    eye_f32 = sb("eye_f32_s", [128, 128], F32)
    triu4 = sb("triu4", [128, 512], BF16)
    iota8 = sb("iota8_s", [128, 128], F32)
    iotaB = sb("iotaB_s", [128, 128], F32)
    half05 = sb("half05", [128, 256], F32)
    W8T = sb("W8T", [128, 128], BF16)
    cb = sb("cb", [128, 4], F32)
    p.tiny = (eye_bf, cb)
    p.finin = cb[:]
    p.finout = io["xsp"][:, 0:4]

    for nm, dst in [("gbias", gbias_sb), ("rmsw4", rmsw4_sb), ("meta", meta_sb),
                    ("ones_col_bf", onescol), ("ones_row_f32", onesrow),
                    ("ones_row_bf", onesrowb), ("eye_bf", eye_bf),
                    ("eye_f32", eye_f32), ("triu4_bf", triu4),
                    ("iota8", iota8), ("iotaB", iotaB), ("cbias", cb)]:
        p.op("y", lambda y, s_=io[nm], d=dst: y.dma_start(out=d[:], in_=s_[:]),
             "DX", 16)
    p.op("g", lambda g: g.dma_start(
        out=gw_sb[:].rearrange("q (k e) -> q k e", k=8),
        in_=io["gatew"].rearrange("(k q) e -> q k e", q=128)), "DW", 16)
    p.op("v", lambda v: v.memset(half05[:], 0.5), "V")
    dx_consts = p.cnt["DX"]
    dw_gw = p.cnt["DW"]

    def emit_ln(xa, pst=None, strips=None, rows_scope=None):
        st = ExitStack()
        xbf = sb("ln_xbf", [128, NDT * 512], BF16, st)
        sq = sb("ln_sq", [128, NDT * 512], BF16, st)
        rm = sb("ln_rm", [1, 512], F32, st)
        rq = sb("ln_rq", [1, 512], F32, st)
        rt = sb("ln_rt", [1, 512], F32, st)
        rrs = sb("ln_rrs", [1, 512], F32, st)
        rmr = sb("ln_rmr", [1, 512], F32, st)
        own = pst is None
        if own:
            pst = pt(st, "ln_ps", [128, 1024])
        ps_sum = pst[0:1, 0:512]
        ps_sq = pst[0:1, 512:1024]
        rstd_b = pst[:, 0:512]
        mr_b = pst[:, 512:1024]
        for s in (range(NSA) if strips is None else strips):
            xa3 = xa[:].rearrange("q (dt t) -> q dt t", dt=NDT)[
                :, :, s * 512:(s + 1) * 512]
            vx = p.op("v", lambda v, a=xa3: v.tensor_copy(
                xbf[:].rearrange("q (dt t) -> q dt t", dt=NDT), a), "V")
            p.wait("s", "V", vx)
            sw = p.op("s", lambda s_: s_.activation(sq[:], xbf[:], AF.Square), "S")
            p.wait("t", "V", vx)
            for k in range(NDT):
                p.op("t", lambda t, k=k: t.matmul(
                    ps_sum, onescol[:], xbf[:, k * 512:(k + 1) * 512],
                    start=(k == 0), stop=(k == NDT - 1)),
                    "T" if k == NDT - 1 else None)
            p.wait("t", "S", sw)
            for k in range(NDT):
                p.op("t", lambda t, k=k: t.matmul(
                    ps_sq, onescol[:], sq[:, k * 512:(k + 1) * 512],
                    start=(k == 0), stop=(k == NDT - 1)),
                    "T" if k == NDT - 1 else None)
            tst = p.cnt["T"]
            p.wait("s", "T", tst)
            p.wait("s", "V", p.cnt["V"])
            p.op("s", lambda s_: s_.mul(rm[:], ps_sum, 1.0 / D), "S")
            sv = p.op("s", lambda s_: s_.mul(rq[:], ps_sq, 1.0 / D), "S")
            p.wait("v", "S", sv)
            p.op("v", lambda v: v.tensor_mul(rt[:], rm[:], rm[:]), None)
            vv = p.op("v", lambda v: v.tensor_sub(rq[:], rq[:], rt[:]), "V")
            p.wait("s", "V", vv)
            s2 = p.op("s", lambda s_: s_.activation(rt[:], rq[:], AF.Sqrt,
                                                    bias=cb[0:1, 0:1]), "S")
            p.wait("v", "S", s2)
            p.op("v", lambda v: v.reciprocal(rrs[:], rt[:]), None)
            vv2 = p.op("v", lambda v: v.tensor_mul(rmr[:], rm[:], rrs[:]), "V")
            p.wait("t", "V", vv2)
            p.op("t", lambda t: t.matmul(rstd_b, onesrow[:], rrs[:],
                                         start=True, stop=True), None)
            tb = p.op("t", lambda t: t.matmul(mr_b, onesrow[:], rmr[:],
                                              start=True, stop=True), "T")
            p.wait("v", "T", tb)
            for k in range(NDT):
                xa_sl = xa[:, k * T + s * 512: k * T + (s + 1) * 512]
                xn_sl = xn[:, k * T + s * 512: k * T + (s + 1) * 512]
                p.op("v", lambda v, a=xa_sl, o=xn_sl: v.tensor_mul(
                    o, a, rstd_b), None)
                p.op("v", lambda v, o=xn_sl: v.tensor_sub(
                    o, o, mr_b), "V" if k == NDT - 1 else None)
            p.wait("t", "V", p.cnt["V"])
        st.close()

    def emit_routing(dump=None):
        st = ExitStack()
        ps_lg = pt(st, "rt_pslg", [128, 128])
        ps_t = pt(st, "rt_pst", [128, 128])
        sfc = sb("rt_sfc", [128, 128], F32, st)
        gs = sb("rt_gs", [128, 64], F32, st)
        rank = sb("rt_rank", [128, 64], F32, st)
        cmp_ = sb("rt_cmp", [128, 16], F32, st)
        tmp = sb("rt_tmp", [128, 128], F32, st)
        tmp2 = sb("rt_tmp2", [128, 128], F32, st)
        m1 = sb("rt_m1", [128, 16], F32, st)
        eq = sb("rt_eq", [128, 128], F32, st)
        tt_ = sb("rt_tt", [128, 128], F32, st)
        idx = sb("rt_idx", [128, 16], F32, st)
        fo1 = sb("rt_fo1", [128, 128], F32, st)
        fo2 = sb("rt_fo2", [128, 128], F32, st)
        w1 = sb("rt_w1", [128, 16], F32, st)
        w2 = sb("rt_w2", [128, 16], F32, st)
        den = sb("rt_den", [128, 16], F32, st)
        W8 = sb("rt_W8", [128, 128], F32, st)

        p.wait("t", "DW", dw_gw)
        for tt in range(NTT):
            for k in range(NDT):
                last = (tt == NTT - 1 and k == NDT - 1)
                p.op("t", lambda t, tt=tt, k=k: t.matmul(
                    ps_lg[:, tt * 8:(tt + 1) * 8],
                    xn[:, k * T + tt * 128: k * T + (tt + 1) * 128],
                    gw_sb[:, k * 8:(k + 1) * 8],
                    start=(k == 0 and tt == 0), stop=(k == NDT - 1)),
                    "T" if last else None)
        p.wait("s", "T", p.cnt["T"])
        ssig = p.op("s", lambda s_: s_.activation(sfc[:], ps_lg[:], AF.Sigmoid), "S")
        p.wait("v", "S", ssig)
        p.wait("v", "DX", dx_consts)
        sfc3 = sfc[:].rearrange("q (tt e) -> q tt e", tt=16)
        p.op("v", lambda v: v.tensor_add(
            sfc3, sfc3, gbias_sb[:].rearrange("q (o e) -> q o e", o=1)
            .broadcast_to([128, 16, 8])), None)
        se = sfc[:].rearrange("q (tt j two) -> q tt j two", tt=16, two=2)
        p.op("v", lambda v: v.tensor_add(
            gs[:].rearrange("q (tt j) -> q tt j", tt=16),
            se[:, :, :, 0:1].rearrange("q tt j o -> q tt (j o)"),
            se[:, :, :, 1:2].rearrange("q tt j o -> q tt (j o)")), None)
        p.op("v", lambda v: v.memset(rank[:], 0.0), None)

        def gsl(ap, j):
            return ap[:].rearrange("q (tt j) -> q tt j", tt=16)[:, :, j:j + 1]
        for jj in range(4):
            rsl = rank[:, jj * 16:(jj + 1) * 16]
            for ii in range(4):
                if ii == jj:
                    continue
                opx = ALU.is_ge if ii < jj else ALU.is_gt
                p.op("v", lambda v, a=gsl(gs, ii), b=gsl(gs, jj), o=opx:
                     v.tensor_tensor(
                         cmp_[:].rearrange("q (tt o) -> q tt o", o=1), a, b, o),
                     None)
                vgap()
                p.op("v", lambda v, r=rsl: v.tensor_add(r, r, cmp_[:]), None)
        vgap()
        p.op("v", lambda v: v.tensor_single_scalar(rank[:], rank[:], 2.0,
                                                   ALU.is_lt), None)
        for j in range(4):
            m8v = tmp2[:].rearrange("q (tt e) -> q tt e", tt=16)[
                :, :, 2 * j:2 * j + 2]
            selv = rank[:, j * 16:(j + 1) * 16].rearrange(
                "q (tt o) -> q tt o", o=1).broadcast_to([128, 16, 2])
            p.op("v", lambda v, o=m8v, a=selv: v.tensor_copy(o, a), None)
        p.op("v", lambda v: v.tensor_mul(tmp[:], sfc[:], tmp2[:]), None)

        def top1(src, fo, wout):
            src3 = src[:].rearrange("q (tt e) -> q tt e", tt=16)
            m13 = m1[:].rearrange("q (tt o) -> q tt o", o=1)
            p.op("v", lambda v: v.tensor_reduce(m13, src3, AX.X, ALU.max), None)
            vgap()
            p.op("v", lambda v: v.tensor_tensor(
                eq[:].rearrange("q (tt e) -> q tt e", tt=16), src3,
                m13.broadcast_to([128, 16, 8]), ALU.is_equal), None)
            p.op("v", lambda v: v.scalar_tensor_tensor(
                tt_[:], eq[:], -1e6, iotaB[:], ALU.mult, ALU.add), None)
            p.op("v", lambda v: v.tensor_reduce(
                idx[:].rearrange("q (tt o) -> q tt o", o=1),
                tt_[:].rearrange("q (tt e) -> q tt e", tt=16), AX.X, ALU.min),
                None)
            vgap()
            p.op("v", lambda v: v.tensor_tensor(
                fo[:].rearrange("q (tt e) -> q tt e", tt=16),
                iota8[:].rearrange("q (tt e) -> q tt e", tt=16),
                idx[:].rearrange("q (tt o) -> q tt o", o=1)
                .broadcast_to([128, 16, 8]), ALU.is_equal), None)
            p.op("v", lambda v: v.tensor_mul(eq[:], fo[:], sfc[:]), None)
            p.op("v", lambda v: v.tensor_reduce(
                wout[:].rearrange("q (tt o) -> q tt o", o=1),
                eq[:].rearrange("q (tt e) -> q tt e", tt=16), AX.X, ALU.add),
                None)
        top1(tmp, fo1, w1)
        p.op("v", lambda v: v.tensor_scalar(
            tmp2[:], fo1[:], -1.0, 1.0, ALU.mult, ALU.add), None)
        p.op("v", lambda v: v.tensor_mul(tmp2[:], tmp2[:], tmp[:]), None)
        top1(tmp2, fo2, w2)
        vgap()
        p.op("v", lambda v: v.tensor_add(den[:], w1[:], w2[:]), None)
        vgap()
        p.op("v", lambda v: v.tensor_single_scalar(den[:], den[:], 1e-20,
                                                   ALU.add), None)
        vgap()
        p.op("v", lambda v: v.reciprocal(den[:], den[:]), None)
        vgap()
        p.op("v", lambda v: v.tensor_single_scalar(den[:], den[:], 1.25,
                                                   ALU.mult), None)
        p.op("v", lambda v: v.tensor_tensor(
            W8[:].rearrange("q (tt e) -> q tt e", tt=16),
            fo1[:].rearrange("q (tt e) -> q tt e", tt=16),
            w1[:].rearrange("q (tt o) -> q tt o", o=1).broadcast_to([128, 16, 8]),
            ALU.mult), None)
        p.op("v", lambda v: v.tensor_tensor(
            eq[:].rearrange("q (tt e) -> q tt e", tt=16),
            fo2[:].rearrange("q (tt e) -> q tt e", tt=16),
            w2[:].rearrange("q (tt o) -> q tt o", o=1).broadcast_to([128, 16, 8]),
            ALU.mult), None)
        p.op("v", lambda v: v.tensor_add(W8[:], W8[:], eq[:]), None)
        if dump is not None:
            dmap = {"sfc": sfc, "gs": gs, "rank": rank, "tmp": tmp,
                    "m1": m1, "idx": idx, "fo1": fo1, "w1": w1, "den": den, "cmp": cmp_,
                    "tmp2": tmp2, "fo2": fo2, "w2": w2}
            dten = dmap[dump]
            dcp = sb("rt_dcp", [128, 128], F32, st)
            ncols = dten.shape[1]
            dv = p.op("v", lambda v, n=ncols, d=dten: v.tensor_copy(
                dcp[:, 0:n], d[:]), "V")
            p.wait("y", "V", dv)
            dten = dcp
            p.op("y", lambda y: y.dma_start(
                out=io["outT"].rearrange("(dt q) t -> q dt t", q=128)[:, 0, 0:ncols],
                in_=dten[:, 0:ncols]), "DOUT", 16)
            p.wait("y", "DOUT", p.cnt["DOUT"])
            p.emit_postamble()
            st.close()
            with nc.Block() as block:
                p.emit(block)
            return None
        p.op("v", lambda v: v.tensor_tensor(
            W8[:].rearrange("q (tt e) -> q tt e", tt=16),
            W8[:].rearrange("q (tt e) -> q tt e", tt=16),
            den[:].rearrange("q (tt o) -> q tt o", o=1).broadcast_to([128, 16, 8]),
            ALU.mult), None)
        vv = p.op("v", lambda v: v.tensor_copy(W8T[:], W8[:]), "V")
        st.close()
        return vv

    def expert_srcs(e):
        if e < 8:
            return (io["fwg"][e], io["fwu"][e], io["fwd"][e])
        h = e - 8
        return (io["swg"][:, h * 512:(h + 1) * 512],
                io["swu"][:, h * 512:(h + 1) * 512],
                io["swd"][h * 512:(h + 1) * 512, :])

    def emit_ffn2(call, xa, w8_ready_v, out_final=False, dbg=False):
        st = ExitStack()
        wgs = [sb(f"f{call}_wg{i}", [128, 4096], BF16, st) for i in range(2)]
        wus = [sb(f"f{call}_wu{i}", [128, 4096], BF16, st) for i in range(2)]
        wds = [sb(f"f{call}_wd{i}", [128, 4096], BF16, st) for i in range(2)]
        Pb = [sb(f"f{call}_P{i}", [128, 1024], BF16, st) for i in range(2)]
        sG = sb(f"f{call}_sG", [128, 1024], BF16, st)
        t2 = sb(f"f{call}_t2", [128, 1024], F32, st)
        wbca = sb(f"f{call}_wbca", [128, 16384], BF16, st)
        xo = sb(f"f{call}_xo", [128, 2048], F32, st) if out_final else None
        ps_g = pt(st, f"f{call}_psg", [128, 1024])
        ps_u = pt(st, f"f{call}_psu", [128, 1024])
        ps_y = pt(st, f"f{call}_psy", [128, 2048])

        desem = ["DE0", "DE1"]
        slot_free_t = [0, 0]
        loaded = [0, 0]

        p.wait("g", "V", p.cnt["V"])
        p.wait("g", "S", p.cnt["S"])
        p.wait("g", "T", p.cnt["T"])

        def load_exp(e):
            sl = e % 2
            p.wait("g", "T", slot_free_t[sl])
            for src, dst, nk in zip(expert_srcs(e), (wgs[sl], wus[sl], wds[sl]),
                                    (8, 8, 4)):
                p.op("g", lambda g, s_=src, d=dst, nk=nk: g.dma_start(
                    out=d[:].rearrange("q (k m) -> q k m", k=nk),
                    in_=s_.rearrange("(k q) m -> q k m", q=128)), desem[sl], 16)
            loaded[sl] = p.cnt[desem[sl]]

        load_exp(0)
        load_exp(1)
        import os as _os3
        if _os3.environ.get("KFFN_DUMP") == "raw0":
            rawb = sb("rawdump", [128, 2048], F32, st)
            p.op("y", lambda y: y.dma_start(
                out=rawb[:].rearrange("q (k m) -> q k m", k=4),
                in_=io["fwg"][0][0:512, :].rearrange("(k q) m -> q k m", q=128)),
                "DOUT", 16)
            p.wait("y", "DOUT", p.cnt["DOUT"])
            p.op("y", lambda y: y.dma_start(
                out=io["outT"].rearrange("(dt q) t -> q dt t", q=128)[:, 0:1, :],
                in_=rawb[:].rearrange("q (o t) -> q o t", o=1)), "DOUT", 16)
            p.wait("y", "DOUT", p.cnt["DOUT"])
            p.emit_postamble()
            st.close()
            with nc.Block() as block:
                p.emit(block)
            return None
        if _os3.environ.get("KFFN_DUMP") == "wg0early":
            p.wait("g", "DE0", loaded[0])
            if _os3.environ.get("KFFN_SLOW"):
                p.wait("g", "V", p.cnt["V"])
                p.wait("g", "T", p.cnt["T"])
                p.wait("g", "S", p.cnt["S"])
                p.wait("g", "DE1", loaded[1])
            p.op("g", lambda g: g.dma_start(
                out=io["outT"].rearrange("(dt q) t -> q dt t", q=128)[:, 0:2, :],
                in_=wgs[0][:].rearrange("q (dt t) -> q dt t", dt=2)), "DOUT", 16)
            p.wait("y", "DOUT", p.cnt["DOUT"])
            p.emit_postamble()
            st.close()
            with nc.Block() as block:
                p.emit(block)
            return None

        p.wait("t", "V", w8_ready_v)
        p.wait("t", "DX", dx_consts)
        last_ev = {0: 0, 1: 0}
        for e in range(8):
            ti = e % 2
            tgt = ps_g if ti == 0 else ps_u
            for half in range(2):
                p.wait("t", "S", last_ev[ti])
                for tt8 in range(8):
                    tt = half * 8 + tt8
                    col = tt * 8 + e
                    p.op("t", lambda t, tg=tgt, c=col, tt8=tt8: t.matmul(
                        tg[:, tt8 * 128: tt8 * 128 + 128],
                        W8T[:, c:c + 1].broadcast_to([128, 128]), eye_bf[:],
                        start=True, stop=True), "T" if tt8 == 7 else None)
                tv = p.cnt["T"]
                base = e * 2048 + half * 1024
                p.wait("s", "T", tv)
                last_ev[ti] = p.op("s", lambda s_, tg=tgt, b=base: s_.copy(
                    wbca[:, b:b + 1024], tg[:, 0:1024]), "S")
        s_wbc = p.cnt["S"]
        if dbg:
            p.wait("g", "S", s_wbc)
            p.op("g", lambda g: g.dma_start(
                out=io["outT"].rearrange("(dt q) t -> q dt t", q=128),
                in_=wbca[:].rearrange("q (dt t) -> q dt t", dt=NDT)), "DOUT", 16)
            p.wait("y", "DOUT", p.cnt["DOUT"])
            p.emit_postamble()
            st.close()
            with nc.Block() as block:
                p.emit(block)
            return None

        v_P = [0, 0]
        s_sg = 0
        yevac_v = 0
        dout_n = 0
        down_t = 0
        pending_down = None

        def emit_down(e, s_i):
            nonlocal down_t, yevac_v, dout_n
            sl = e % 2
            pp = e % 2
            p.wait("t", "V", v_P[pp])
            p.wait("t", "V", yevac_v)
            for kt in range(4):
                for dd in range(NDT):
                    last = (kt == 3 and dd == NDT - 1)
                    p.op("t", lambda t, kt=kt, dd=dd, w=wds[sl], pb=Pb[pp]:
                         t.matmul(
                             ps_y[:, dd * 256:(dd + 1) * 256],
                             w[:, kt * 1024 + dd * 128: kt * 1024 + dd * 128 + 128],
                             pb[:, kt * 256:(kt + 1) * 256],
                             start=(kt == 0 and dd % 2 == 0), stop=(kt == 3)),
                         "T" if last else None)
            down_t = p.cnt["T"]
            p.wait("v", "T", down_t)
            xa3 = xa[:].rearrange("q (dt t) -> q dt t", dt=NDT)[
                :, :, s_i * 256:(s_i + 1) * 256]
            psy3 = ps_y[:].rearrange("q (dt t) -> q dt t", dt=NDT)
            if out_final and e == NEXP - 1:
                p.wait("v", "DOUT", dout_n)
                yevac_v = p.op("v", lambda v, a=xa3: v.tensor_add(
                    xo[:].rearrange("q (dt t) -> q dt t", dt=NDT), a, psy3), "V")
                p.wait("y", "V", yevac_v)
                out3 = io["outT"].rearrange("(dt q) t -> q dt t", q=128)[
                    :, :, s_i * 256:(s_i + 1) * 256]
                dout_n = p.op("y", lambda y, o=out3: y.dma_start(
                    out=o, in_=xo[:].rearrange("q (dt t) -> q dt t", dt=NDT)),
                    "DOUT", 16)
            else:
                yevac_v = p.op("v", lambda v, a=xa3: v.tensor_add(a, a, psy3), "V")
            return down_t

        import os as _os2
        nexp_run = int(_os2.environ.get("KFFN_NEXP") or NEXP)
        for e in range(nexp_run):
            sl = e % 2
            pp = e % 2
            for s_i in range(NSF):
                p.wait("t", desem[sl], loaded[sl])
                p.wait("t", "V", v_P[pp])
                p.wait("t", "S", max(s_sg, s_wbc))
                for it in range(4):
                    for k in range(NDT):
                        p.op("t", lambda t, it=it, k=k, w=wgs[sl], s_i=s_i:
                             t.matmul(
                                 ps_g[:, it * 256:(it + 1) * 256],
                                 w[:, k * 512 + it * 128: k * 512 + it * 128 + 128],
                                 xn[:, k * T + s_i * 256: k * T + (s_i + 1) * 256],
                                 start=(k == 0 and it % 2 == 0),
                                 stop=(k == NDT - 1)), None)
                for it in range(4):
                    for k in range(NDT):
                        last = (it == 3 and k == NDT - 1)
                        p.op("t", lambda t, it=it, k=k, w=wus[sl], s_i=s_i:
                             t.matmul(
                                 ps_u[:, it * 256:(it + 1) * 256],
                                 w[:, k * 512 + it * 128: k * 512 + it * 128 + 128],
                                 xn[:, k * T + s_i * 256: k * T + (s_i + 1) * 256],
                                 start=(k == 0 and it % 2 == 0),
                                 stop=(k == NDT - 1)), "T" if last else None)
                t_gu = p.cnt["T"]
                if pending_down is not None:
                    emit_down(*pending_down)
                    if pending_down[1] == NSF - 1:
                        eprev = pending_down[0]
                        slot_free_t[eprev % 2] = down_t
                        if eprev + 2 < NEXP:
                            load_exp(eprev + 2)
                pending_down = (e, s_i)
                p.wait("s", "T", t_gu)
                s_sg = p.op("s", lambda s_: s_.activation(sG[:], ps_g[:],
                                                          AF.Silu), "S")
                p.wait("v", "T", t_gu)
                if e < 8:
                    wsl = wbca[:, e * 2048 + s_i * 256: e * 2048 + (s_i + 1) * 256]
                else:
                    wsl = half05[:]
                p.op("v", lambda v, w=wsl: v.tensor_tensor(
                    t2[:].rearrange("q (it t) -> q it t", it=4),
                    ps_u[:].rearrange("q (it t) -> q it t", it=4),
                    w.rearrange("q (o t) -> q o t", o=1)
                    .broadcast_to([128, 4, 256]), ALU.mult), None)
                p.wait("v", "S", s_sg)
                p.wait("v", "T", down_t)
                v_P[pp] = p.op("v", lambda v, d=Pb[pp]: v.tensor_mul(
                    d[:], sG[:], t2[:]), "V")
                dw = _os2.environ.get("KFFN_DUMP")
                if dw and e == 0 and s_i == 0:
                    if dw in ("wg0", "wu0", "wd0"):
                        src = {"wg0": wgs[0], "wu0": wus[0], "wd0": wds[0]}[dw]
                        p.wait("g", "V", p.cnt["V"])
                        p.wait("g", "S", p.cnt["S"])
                        p.wait("g", "T", p.cnt["T"])
                        p.wait("g", "DE0", loaded[0])
                        p.op("g", lambda g, a=src: g.dma_start(
                            out=io["outT"].rearrange(
                                "(dt q) t -> q dt t", q=128)[:, 0:2, :],
                            in_=a[:].rearrange("q (dt t) -> q dt t", dt=2)),
                            "DOUT", 16)
                        p.wait("y", "DOUT", p.cnt["DOUT"])
                        p.emit_postamble()
                        st.close()
                        with nc.Block() as block:
                            p.emit(block)
                        return None
                    src = {"sg": sG, "t2": t2, "pb": Pb[0]}[dw]
                    p.wait("g", "V", p.cnt["V"])
                    p.wait("g", "S", p.cnt["S"])
                    p.op("g", lambda g, a=src: g.dma_start(
                        out=io["outT"].rearrange("(dt q) t -> q dt t", q=128)[
                            :, 0:8, 0:128],
                        in_=a[:].rearrange("q (dt t) -> q dt t", dt=8)),
                        "DOUT", 16)
                    p.wait("y", "DOUT", p.cnt["DOUT"])
                    p.emit_postamble()
                    st.close()
                    with nc.Block() as block:
                        p.emit(block)
                    return None
        if pending_down is not None:
            emit_down(*pending_down)
        p.wait("t", "V", yevac_v)
        st.close()
        return dout_n

    stA = top.enter_context(ExitStack())
    xa = sb("xa", [128, NDT * T], F32, stA)
    p.op("y", lambda y: y.dma_start(
        out=xa[:].rearrange("q (dt t) -> q dt t", dt=NDT),
        in_=io["xT"].rearrange("(dt q) t -> q dt t", q=128)), "DX", 16)
    dx_x = p.cnt["DX"]
    for e_ in ("t", "v", "s"):
        p.wait(e_, "DX", dx_x)
    import os as _osw
    if phases == "wload":
        mode = _osw.environ.get("WLOAD_MODE", "bare")
        if mode in ("ln", "lnrt"):
            emit_ln(xa)
        if mode == "lnrt":
            emit_routing()
        wl = sb("wl_wg0", [128, 4096], BF16, stA)
        p.op("g", lambda g: g.dma_start(
            out=wl[:].rearrange("q (k m) -> q k m", k=8),
            in_=io["fwg"][0].rearrange("(k q) m -> q k m", q=128)), "DE0", 16)
        p.wait("g", "DE0", p.cnt["DE0"])
        if mode != "bare":
            p.wait("g", "V", p.cnt["V"])
            p.wait("g", "S", p.cnt["S"])
            p.wait("g", "T", p.cnt["T"])
        p.op("g", lambda g: g.dma_start(
            out=io["outT"].rearrange("(dt q) t -> q dt t", q=128)[:, 0:2, :],
            in_=wl[:].rearrange("q (dt t) -> q dt t", dt=2)), "DOUT", 16)
        p.wait("y", "DOUT", p.cnt["DOUT"])
        with nc.Block() as block:
            p.emit(block)
        return
    emit_ln(xa)
    if phases == "ln1":
        finish_dbg(nc, p, io, xn_bf=xn)
        return
    import os as _os
    w8v = emit_routing(dump=_os.environ.get("RT_DUMP") or None)
    if _os.environ.get("RT_DUMP"):
        return
    if phases == "rt":
        finish_dbg(nc, p, io, xn_bf=W8T)
        return
    emit_ffn2(1, xa, w8v, dbg=(phases == "wbc"))
    if phases == "wbc" or _os.environ.get("KFFN_DUMP"):
        return
    if phases == "ffn1":
        finish_dbg(nc, p, io, xa_f32=xa)
        return
    emit_ln(xa)
    if phases == "ln2":
        finish_dbg(nc, p, io, xn_bf=xn)
        return
    p.wait("y", "V", p.cnt["V"])
    p.op("y", lambda y: y.dma_start(out=io["xsp"][:], in_=xa[:]), "DSP", 16)
    xsp_done = p.cnt["DSP"]
    snapA = p.barrier()
    for e_ in P.ENG:
        p.wait(e_, "DSP", xsp_done)
    stA.close()

    stB = top.enter_context(ExitStack())
    wq_sb = sb("wq_sb", [128, 8192], BF16, stB)
    wk_sb = sb("wk_sb", [128, 8192], BF16, stB)
    wv_sb = sb("wv_sb", [128, 16384], BF16, stB)
    wgk1_sb = sb("wgk1_sb", [128, 128], BF16, stB)
    wgk2_sb = sb("wgk2_sb", [17, 1024], BF16, stB)
    qT = sb("qT", [128, 4096], BF16, stB)
    kT = sb("kT", [128, 4096], BF16, stB)
    vbs = sb("vbs", [128, 8192], BF16, stB)
    cum = sb("cum", [128, 4096], F32, stB)
    u1e = sb("u1e", [17, 512], BF16, stB)
    S_sb = sb("S_sb", [128, 4096], F32, stB)
    S_bf = sb("S_bf", [128, 4096], BF16, stB)
    BP = sb("BP", [128, 8], F32, stB)
    colB = sb("colB", [128, 8], F32, stB)
    Ebl = sb("Ebl", [128, 8], F32, stB)
    Eslp = sb("Eslp", [128, 8], F32, stB)
    EBP = sb("EBP", [128, 8], F32, stB)
    eqb = [sb(f"eqb{i}", [128, 1024], BF16, stB) for i in range(2)]
    qg = sb("qg", [128, 1024], BF16, stB)
    kd = sb("kd", [128, 1024], BF16, stB)
    kdf = sb("kdf", [128, 1024], F32, stB)
    kgp = sb("kgp", [128, 1024], BF16, stB)
    qtc = sb("qtc", [128, 1024], BF16, stB)
    Am = sb("Am", [128, 512], BF16, stB)
    kgt = sb("kgt", [128, 1024], BF16, stB)
    o_ch = sb("o_ch", [128, 2048], BF16, stB)
    zero128 = sb("zero128", [128, 128], F32, stB)
    ps = pt(stB, "ps_scan", [128, 4096])

    p.op("v", lambda v: v.memset(zero128[:], 0.0), None)
    p.op("v", lambda v: v.memset(S_sb[:], 0.0), None)
    p.op("v", lambda v: v.memset(S_bf[:], 0.0), None)
    p.op("v", lambda v: v.memset(BP[:], 0.0), "V")
    p.op("g", lambda g: g.dma_start(out=u1e[16:17, :],
                                    in_=io["ones_row512_bf"][:]), "DW", 16)
    for nm, dst in [("wq", wq_sb), ("wk", wk_sb), ("wv", wv_sb),
                    ("wgk1", wgk1_sb)]:
        p.op("g", lambda g, s_=io[nm], d=dst: g.dma_start(
            out=d[:].rearrange("q (k m) -> q k m", k=8),
            in_=s_.rearrange("(k q) m -> q k m", q=128)), "DW", 16)
    p.op("g", lambda g: g.dma_start(out=wgk2_sb[:], in_=io["wgk2e"][:]), "DW", 16)
    dw_attn = p.cnt["DW"]

    stt_v = p.cnt["V"]
    sbf_s = p.cnt["S"]
    v_init = p.cnt["V"]
    t_lastchunk = 0
    dsp_chunk = p.cnt["DSP"]
    v_cumread = 0
    t_A = t_o = t_tr = t_ds = 0
    rot = [0, 512, 1024, 1536]
    evac_hist = [(None, 0)] * 4

    for s_i in range(NSA):
        p.wait("t", "DW", dw_attn)
        p.wait("t", "V", stt_v)
        p.wait("v", "T", t_lastchunk)
        p.wait("s", "T", t_lastchunk)
        ri = 0

        def proj_tile(emit_mms, evac_engine, evac_fn):
            nonlocal ri
            col = rot[ri % 4]
            sem, cv = evac_hist[ri % 4]
            if sem is not None:
                p.wait("t", sem, cv)
            emit_mms(col)
            tmm = p.cnt["T"]
            p.wait(evac_engine, "T", tmm)
            ev = evac_fn(col)
            evac_hist[ri % 4] = ({"v": "V", "s": "S"}[evac_engine], ev)
            ri += 1

        for wsb, dst in ((wq_sb, qT), (wk_sb, kT)):
            for mt in range(8):
                def mk(col, wsb=wsb, mt=mt, s_i=s_i):
                    for k in range(NDT):
                        p.op("t", lambda t, c=col, k=k, wsb=wsb, mt=mt, s_i=s_i:
                             t.matmul(
                                 ps[:, c:c + 512],
                                 wsb[:, k * 1024 + mt * 128: k * 1024 + mt * 128 + 128],
                                 xn[:, k * T + s_i * 512: k * T + (s_i + 1) * 512],
                                 start=(k == 0), stop=(k == NDT - 1)),
                             "T" if k == NDT - 1 else None)

                def ev(col, dst=dst, mt=mt):
                    return p.op("v", lambda v, c=col: v.tensor_copy(
                        dst[:, mt * 512:(mt + 1) * 512], ps[:, c:c + 512]), "V")
                proj_tile(mk, "v", ev)
        for tt2 in range(4):
            for nn in range(4):
                def mk(col, tt2=tt2, nn=nn, s_i=s_i):
                    for k in range(NDT):
                        p.op("t", lambda t, c=col, k=k, tt2=tt2, nn=nn, s_i=s_i:
                             t.matmul(
                                 ps[:, c:c + 512],
                                 xn[:, k * T + s_i * 512 + tt2 * 128:
                                    k * T + s_i * 512 + tt2 * 128 + 128],
                                 wv_sb[:, k * 2048 + nn * 512: k * 2048 + (nn + 1) * 512],
                                 start=(k == 0), stop=(k == NDT - 1)),
                             "T" if k == NDT - 1 else None)

                def ev(col, tt2=tt2, nn=nn):
                    return p.op("s", lambda s_, c=col: s_.copy(
                        vbs[:, tt2 * 2048 + nn * 512: tt2 * 2048 + (nn + 1) * 512],
                        ps[:, c:c + 512]), "S")
                proj_tile(mk, "s", ev)
        for k in range(NDT):
            p.op("t", lambda t, k=k, s_i=s_i: t.matmul(
                ps[0:16, 2048:2560], wgk1_sb[:, k * 16:(k + 1) * 16],
                xn[:, k * T + s_i * 512: k * T + (s_i + 1) * 512],
                start=(k == 0), stop=(k == NDT - 1)),
                "T" if k == NDT - 1 else None)
        tu1 = p.cnt["T"]
        p.wait("v", "T", tu1)
        vu1 = p.op("v", lambda v: v.tensor_copy(u1e[0:16, :], ps[0:16, 2048:2560]),
                   "V")
        p.wait("t", "V", vu1)
        p.wait("s", "V", v_cumread)
        for mt in range(8):
            def mk(col, mt=mt):
                p.op("t", lambda t, c=col, mt=mt: t.matmul(
                    ps[:, c:c + 512], wgk2_sb[0:17, mt * 128:(mt + 1) * 128],
                    u1e[0:17, :], start=True, stop=True), "T")

            def ev(col, mt=mt):
                return p.op("s", lambda s_, c=col, mt=mt: s_.activation(
                    cum[:, mt * 512:(mt + 1) * 512], ps[:, c:c + 512],
                    AF.Sigmoid), "S")
            proj_tile(mk, "s", ev)
        sln = p.op("s", lambda s_: s_.activation(cum[:], cum[:], AF.Ln), "S")
        p.wait("v", "S", sln)
        for dt_ in range(NDT):
            for c in range(4):
                sl = cum[:, dt_ * 512 + c * 128: dt_ * 512 + (c + 1) * 128]
                last = (dt_ == NDT - 1 and c == 3)
                p.op("v", lambda v, sl=sl: v.tensor_tensor_scan(
                    sl, sl, zero128[:], 0.0, ALU.add, ALU.add),
                    "V" if last else None)

        for c in range(4):
            par = c % 2
            cum_ch = cum[:].rearrange("q (dt t) -> q dt t", dt=NDT)[
                :, :, c * 128:(c + 1) * 128]
            lastcol = cum[:].rearrange("q (dt t) -> q dt t", dt=NDT)[
                :, :, c * 128 + 127: c * 128 + 128]
            vcolB = p.op("v", lambda v, a=lastcol: v.tensor_single_scalar(
                colB[:].rearrange("q (e o) -> q e o", o=1), a, 1.0 / 16.0,
                ALU.mult), "V")
            p.wait("s", "V", vcolB)
            sEBP = p.op("s", lambda s_: s_.activation(EBP[:], BP[:], AF.Exp), "S")
            p.wait("v", "S", sEBP)
            vBP = p.op("v", lambda v: v.tensor_add(BP[:], BP[:], colB[:]), "V")
            sEsl = p.op("s", lambda s_: s_.activation(
                Eslp[:], colB[:], AF.Exp, scale=1.0, bias=cb[:, 1:2]), "S")
            sEbl = p.op("s", lambda s_: s_.activation(Ebl[:], colB[:], AF.Exp),
                        "S")
            seq = p.op("s", lambda s_, a=cum_ch, b=eqb[par]: s_.activation(
                b[:].rearrange("q (dt t) -> q dt t", dt=NDT), a, AF.Exp,
                scale=1.0 / 16.0, bias=cb[:, 1:2]), "S")
            qT_ch = qT[:].rearrange("q (dt t) -> q dt t", dt=NDT)[
                :, :, c * 128:(c + 1) * 128]
            kT_ch = kT[:].rearrange("q (dt t) -> q dt t", dt=NDT)[
                :, :, c * 128:(c + 1) * 128]
            p.wait("v", "T", t_o)
            p.wait("v", "T", t_A)
            p.wait("v", "T", t_tr)
            p.wait("v", "S", seq)
            p.op("v", lambda v, a=qT_ch, b=eqb[par]: v.tensor_tensor(
                qg[:].rearrange("q (dt t) -> q dt t", dt=NDT), a,
                b[:].rearrange("q (dt t) -> q dt t", dt=NDT), ALU.mult), None)
            p.op("v", lambda v, b=eqb[par]: v.reciprocal(kdf[:], b[:]), None)
            p.op("v", lambda v, a=kT_ch: v.tensor_tensor(
                kd[:].rearrange("q (dt t) -> q dt t", dt=NDT), a,
                kdf[:].rearrange("q (dt t) -> q dt t", dt=NDT), ALU.mult), None)
            p.op("v", lambda v, a=kT_ch: v.tensor_tensor(
                kgp[:].rearrange("q (dt t) -> q dt t", dt=NDT), a,
                kdf[:].rearrange("q (dt t) -> q dt t", dt=NDT), ALU.mult), None)
            p.wait("v", "S", sEsl)
            for dt_ in range(NDT):
                p.op("v", lambda v, dt_=dt_: v.tensor_single_scalar(
                    kgp[:, dt_ * 128:(dt_ + 1) * 128],
                    kgp[:, dt_ * 128:(dt_ + 1) * 128],
                    Eslp[:, dt_:dt_ + 1], ALU.mult), None)
            p.wait("v", "DSP", dsp_chunk)
            for dt_ in range(NDT):
                last = dt_ == NDT - 1
                p.op("v", lambda v, dt_=dt_: v.tensor_single_scalar(
                    qtc[:, dt_ * 128:(dt_ + 1) * 128],
                    qg[:, dt_ * 128:(dt_ + 1) * 128],
                    EBP[:, dt_:dt_ + 1], ALU.mult), "V" if last else None)
            v_elem = p.cnt["V"]
            v_cumread = v_elem
            p.wait("t", "V", v_elem)
            for h in range(HEADS):
                for j in range(2):
                    dt_ = h * 2 + j
                    last = (h == HEADS - 1 and j == 1)
                    p.op("t", lambda t, h=h, dt_=dt_: t.matmul(
                        ps[:, 2048 + h * 128: 2048 + (h + 1) * 128],
                        kd[:, dt_ * 128:(dt_ + 1) * 128],
                        qg[:, dt_ * 128:(dt_ + 1) * 128],
                        start=(dt_ == 0), stop=(dt_ % 2 == 1)),
                        "T" if last else None)
            t_A = p.cnt["T"]
            p.wait("v", "T", t_A)
            amv = p.op("v", lambda v: v.tensor_tensor(
                Am[:], ps[:, 2048:2560], triu4[:], ALU.mult), "V")
            p.wait("t", "V", amv)
            p.wait("t", "V", stt_v)
            p.wait("t", "S", sbf_s)
            for h in range(HEADS):
                for vt in range(4):
                    p.op("t", lambda t, h=h, vt=vt, c=c: t.matmul(
                        ps[:, (h * 4 + vt) * 128:(h * 4 + vt + 1) * 128],
                        vbs[:, c * 2048 + h * 512 + vt * 128:
                            c * 2048 + h * 512 + vt * 128 + 128],
                        Am[:, h * 128:(h + 1) * 128],
                        start=(vt == 0), stop=False), None)
                for dt2 in range(2):
                    for vt in range(4):
                        last = (h == HEADS - 1 and dt2 == 1 and vt == 3)
                        p.op("t", lambda t, h=h, dt2=dt2, vt=vt: t.matmul(
                            ps[:, (h * 4 + vt) * 128:(h * 4 + vt + 1) * 128],
                            S_bf[:, (h * 2 + dt2) * 512 + vt * 128:
                                 (h * 2 + dt2) * 512 + vt * 128 + 128],
                            qg[:, (h * 2 + dt2) * 128:(h * 2 + dt2 + 1) * 128],
                            start=False, stop=(dt2 == 1 and vt == 3)),
                            "T" if last else None)
            t_o = p.cnt["T"]
            pstr = ps[:, 2560:3072].bitcast(BF16)
            for dt_ in range(NDT):
                last = dt_ == NDT - 1
                p.op("t", lambda t, dt_=dt_, pstr=pstr: t.transpose(
                    pstr[:, dt_ * 128:(dt_ + 1) * 128],
                    kgp[:, dt_ * 128:(dt_ + 1) * 128], eye_bf[:]),
                    "T" if last else None)
            t_tr = p.cnt["T"]
            p.wait("v", "T", t_o)
            p.wait("v", "DSP", dsp_chunk)
            oev = p.op("v", lambda v: v.tensor_copy(o_ch[:], ps[:, 0:2048]), "V")
            p.wait("y", "V", oev)
            for h in range(HEADS):
                dsto = io["olb"].rearrange("q (h s4 vt u) -> q h s4 vt u",
                                           h=4, s4=4, vt=4)[
                    :, h, s_i, :, c * 128:(c + 1) * 128]
                p.op("y", lambda y, d=dsto, h=h: y.dma_start(
                    out=d, in_=o_ch[:, h * 512:(h + 1) * 512]
                    .rearrange("q (vt t) -> q vt t", vt=4)), "DSP", 16)
            for h in range(HEADS):
                dstq = io["qtb"].rearrange("q (h dt2 s4 u) -> q h dt2 s4 u",
                                           h=4, dt2=2, s4=4)[
                    :, h, :, s_i, c * 128:(c + 1) * 128]
                p.op("y", lambda y, d=dstq, h=h: y.dma_start(
                    out=d, in_=qtc[:, h * 256:(h + 1) * 256]
                    .rearrange("q (dt2 t) -> q dt2 t", dt2=2)), "DSP", 16)
            dsp_chunk = p.cnt["DSP"]
            p.wait("s", "T", t_tr)
            skgt = p.op("s", lambda s_, pstr=pstr: s_.copy(
                kgt[:], pstr[:, 0:1024]), "S")
            p.wait("t", "V", oev)
            p.wait("t", "V", amv)
            p.wait("t", "S", skgt)
            for h in range(HEADS):
                for dt2 in range(2):
                    i = h * 2 + dt2
                    last = i == 7
                    p.op("t", lambda t, h=h, i=i, c=c: t.matmul(
                        ps[:, i * 512:(i + 1) * 512],
                        kgt[:, i * 128:(i + 1) * 128],
                        vbs[:, c * 2048 + h * 512: c * 2048 + h * 512 + 512],
                        start=True, stop=True), "T" if last else None)
            t_ds = p.cnt["T"]
            t_lastchunk = t_ds
            p.wait("v", "T", t_ds)
            p.wait("v", "S", sEbl)
            for i in range(8):
                last = i == 7
                p.op("v", lambda v, i=i: v.scalar_tensor_tensor(
                    S_sb[:, i * 512:(i + 1) * 512],
                    S_sb[:, i * 512:(i + 1) * 512],
                    Ebl[:, i:i + 1], ps[:, i * 512:(i + 1) * 512],
                    ALU.mult, ALU.add), "V" if last else None)
            stt_v = p.cnt["V"]
            p.wait("s", "V", stt_v)
            sbf_s = p.op("s", lambda s_: s_.copy(S_bf[:], S_sb[:]), "S")

    import os as _oss
    if _oss.environ.get("KSCAN_RAW") and phases == "scan":
        p.wait("y", "V", stt_v)
        p.op("y", lambda y: y.dma_start(
            out=io["outT"].rearrange("(dt q) t -> q dt t", q=128)[:, 0:2, :],
            in_=S_sb[:].rearrange("q (dt t) -> q dt t", dt=2)), "DOUT", 16)
        p.wait("y", "DOUT", p.cnt["DOUT"])
        with nc.Block() as block:
            p.emit(block)
        return
    p.wait("y", "V", stt_v)
    cin128 = io["cin"].rearrange("o (q f) -> (o q) f", q=128)
    p.op("y", lambda y: y.dma_start(out=cin128[:, 0:4096], in_=S_sb[:]),
         "DSP", 16)
    p.op("y", lambda y: y.dma_start(out=cin128[:, 4096:4104], in_=BP[:]),
         "DSP", 16)
    cin_done = p.cnt["DSP"]
    p.wait("g", "DSP", cin_done)
    p.op("g", lambda g: g.collective_compute(
        "AllGather", ALU.bypass, replica_groups=[[0, 1, 2, 3], [4, 5, 6, 7]],
        ins=[io["cin"][:]], outs=[io["cout"][:]]), "G")
    g_done = p.cnt["G"]
    if phases == "scan":
        finish_dbg(nc, p, io, wait_g=g_done)
        return
    snapB = p.barrier()
    for e_ in P.ENG:
        p.wait(e_, "DSP", p.cnt["DSP"])
    stB.close()

    stCD = ExitStack()
    xa2 = sb("xa2", [128, NDT * T], F32, stCD)
    S_inb = sb("S_inb", [128, 4096], BF16, stCD)

    p.op("y", lambda y: y.dma_start(out=xa2[:], in_=io["xsp"][:]), "DSI2", 16)
    xa2_done = p.cnt["DSI2"]

    stJ = top.enter_context(ExitStack())
    S_in = sb("S_in", [128, 4096], F32, stJ)
    Sj = sb("Sj", [128, 4096], F32, stJ)
    Bj = sb("Bj", [128, 8], F32, stJ)
    accB = sb("accB", [128, 8], F32, stJ)
    Dj = sb("Dj", [128, 8], F32, stJ)
    p.op("v", lambda v: v.memset(S_in[:], 0.0), None)
    p.op("v", lambda v: v.memset(accB[:], 0.0), "V")
    p.wait("y", "G", g_done)
    for j in (2, 1, 0):
        co = io["cout"][j:j + 1, :].rearrange("o (q f) -> (o q) f", q=128)
        p.wait("y", "V", p.cnt["V"])
        p.op("y", lambda y, co=co: y.dma_start(out=Sj[:], in_=co[:, 0:4096]),
             "DCO", 16)
        p.op("y", lambda y, co=co: y.dma_start(out=Bj[:], in_=co[:, 4096:4104]),
             "DCO", 16)
        dco = p.cnt["DCO"]
        p.wait("s", "DCO", dco)
        p.wait("s", "V", p.cnt["V"])
        sdj = p.op("s", lambda s_: s_.activation(Dj[:], accB[:], AF.Exp), "S")
        p.wait("v", "S", sdj)
        p.wait("v", "DCO", dco)
        p.op("v", lambda v, j=j: v.tensor_single_scalar(
            Dj[:], Dj[:], meta_sb[:, j:j + 1], ALU.mult), None)
        p.op("v", lambda v: v.memset(gapb[:], 0.0), None)
        for i in range(8):
            p.op("v", lambda v, i=i: v.scalar_tensor_tensor(
                S_in[:, i * 512:(i + 1) * 512], Sj[:, i * 512:(i + 1) * 512],
                Dj[:, i:i + 1], S_in[:, i * 512:(i + 1) * 512],
                ALU.mult, ALU.add), None)
        p.op("v", lambda v, j=j: v.scalar_tensor_tensor(
            accB[:], Bj[:], meta_sb[:, j:j + 1], accB[:], ALU.mult, ALU.add),
            "V")
    vcomb = p.op("v", lambda v: v.tensor_copy(S_inb[:], S_in[:]), "V")
    p.wait("y", "V", vcomb)
    p.wait("t", "V", vcomb)
    stJ.close()

    stC = top.enter_context(ExitStack())
    wgh = sb("wgh", [128, 8192], BF16, stC)
    woh = sb("woh", [128, 8192], BF16, stC)
    oh = sb("oh", [128, 2048], BF16, stC)
    qth = sb("qth", [128, 1024], BF16, stC)
    of = sb("of", [128, 2048], F32, stC)
    sqb = sb("sqb", [128, 2048], BF16, stC)
    sgb = sb("sgb", [128, 4096], BF16, stC)
    zh = sb("zh", [128, 4096], BF16, stC)
    rr1 = sb("rr1", [1, 512], F32, stC)
    rr = sb("rr", [1, 512], F32, stC)
    ps2 = pt(stC, "ps_post", [128, 4096])

    dsis = ["DSI0", "DSI1"]
    dsi_used_v = [0, 0]
    prev_tcorr = 0
    ln3_first = True
    for hh in range(2):
        p.wait("g", "T", p.cnt["T"])
        p.wait("g", "V", p.cnt["V"])
        p.op("g", lambda g, hh=hh: g.dma_start(
            out=wgh[:].rearrange("q (k m) -> q k m", k=8),
            in_=io["wg"][:, hh * 1024:(hh + 1) * 1024]
            .rearrange("(k q) m -> q k m", q=128)), "DW", 16)
        p.op("g", lambda g, hh=hh: g.dma_start(
            out=woh[:].rearrange("q (k m) -> q k m", k=8),
            in_=io["wo"][hh * 1024:(hh + 1) * 1024, :]
            .rearrange("(k q) m -> q k m", q=128)), "DW", 16)
        dwh = p.cnt["DW"]
        p.wait("v", "DW", dwh)
        for kt in range(8):
            last = kt == 7
            p.op("v", lambda v, kt=kt: v.tensor_single_scalar(
                woh[:, kt * 1024:(kt + 1) * 1024],
                woh[:, kt * 1024:(kt + 1) * 1024],
                rmsw4_sb[:, kt % 4:kt % 4 + 1], ALU.mult),
                "V" if last else None)
        vfold = p.cnt["V"]
        for s_i in range(NSA):
            for hl in range(2):
                h = hh * 2 + hl
                sem = dsis[(s_i * 2 + hl) % 2]
                p.wait("y", "V", dsi_used_v[(s_i * 2 + hl) % 2])
                p.wait("y", "T", prev_tcorr)
                oin = io["olb"].rearrange("q (h s4 u) -> q h s4 u", h=4, s4=4)[
                    :, h, s_i, :]
                p.op("y", lambda y, a=oin: y.dma_start(out=oh[:], in_=a),
                     sem, 16)
                qin = io["qtb"].rearrange("q (h dt2 s4 u) -> q h dt2 s4 u",
                                          h=4, dt2=2, s4=4)[:, h, :, s_i, :]
                p.op("y", lambda y, a=qin: y.dma_start(
                    out=qth[:].rearrange("q (dt2 u) -> q dt2 u", dt2=2), in_=a),
                    sem, 16)
                dsi = p.cnt[sem]
                p.wait("t", sem, dsi)
                for vt in range(4):
                    for dt2 in range(2):
                        last = (vt == 3 and dt2 == 1)
                        p.op("t", lambda t, h=h, vt=vt, dt2=dt2: t.matmul(
                            ps2[:, vt * 512:(vt + 1) * 512],
                            S_inb[:, (h * 2 + dt2) * 512 + vt * 128:
                                  (h * 2 + dt2) * 512 + vt * 128 + 128],
                            qth[:, dt2 * 512:(dt2 + 1) * 512],
                            start=(dt2 == 0), stop=(dt2 == 1)),
                            "T" if last else None)
                tcorr = p.cnt["T"]
                prev_tcorr = tcorr
                p.wait("v", "T", tcorr)
                p.wait("v", sem, dsi)
                vof = p.op("v", lambda v: v.tensor_add(
                    of[:], oh[:], ps2[:, 0:2048]), "V")
                dsi_used_v[(s_i * 2 + hl) % 2] = vof
                p.wait("s", "V", vof)
                ssq = p.op("s", lambda s_: s_.activation(sqb[:], of[:],
                                                         AF.Square), "S")
                p.wait("t", "S", ssq)
                for vt in range(4):
                    p.op("t", lambda t, vt=vt: t.matmul(
                        ps2[0:1, 2048:2560], onescol[:],
                        sqb[:, vt * 512:(vt + 1) * 512],
                        start=(vt == 0), stop=(vt == 3)),
                        "T" if vt == 3 else None)
                tms = p.cnt["T"]
                p.wait("s", "T", tms)
                srr = p.op("s", lambda s_: s_.activation(
                    rr1[:], ps2[0:1, 2048:2560], AF.Sqrt, scale=1.0 / HDV,
                    bias=cb[0:1, 0:1]), "S")
                p.wait("v", "S", srr)
                vrr = p.op("v", lambda v: v.reciprocal(rr[:], rr1[:]), "V")
                p.wait("t", "V", vrr)
                p.op("t", lambda t: t.matmul(
                    ps2[:, 2560:3072], onesrow[:], rr[:],
                    start=True, stop=True), "T")
                p.wait("t", "V", vof)
                for vt in range(4):
                    for k in range(NDT):
                        last = (vt == 3 and k == NDT - 1)
                        p.op("t", lambda t, vt=vt, k=k, hl=hl, s_i=s_i:
                             t.matmul(
                                 ps2[:, vt * 512:(vt + 1) * 512],
                                 wgh[:, k * 1024 + hl * 512 + vt * 128:
                                     k * 1024 + hl * 512 + vt * 128 + 128],
                                 xn[:, k * T + s_i * 512: k * T + (s_i + 1) * 512],
                                 start=(k == 0), stop=(k == NDT - 1)),
                             "T" if last else None)
                tgt_ = p.cnt["T"]
                p.wait("s", "T", tgt_)
                ssg = p.op("s", lambda s_, hl=hl: s_.activation(
                    sgb[:, hl * 2048:(hl + 1) * 2048], ps2[:, 0:2048],
                    AF.Silu), "S")
                p.wait("v", "S", ssg)
                for vt in range(4):
                    p.op("v", lambda v, vt=vt: v.tensor_mul(
                        of[:, vt * 512:(vt + 1) * 512],
                        of[:, vt * 512:(vt + 1) * 512],
                        ps2[:, 2560:3072]), None)
                p.op("v", lambda v, hl=hl: v.tensor_tensor(
                    zh[:, hl * 2048:(hl + 1) * 2048], of[:, 0:2048],
                    sgb[:, hl * 2048:(hl + 1) * 2048], ALU.mult), "V")
            vz = p.cnt["V"]
            p.wait("t", "V", max(vz, vfold))
            if hh == 0 and s_i == 0:
                p.wait("t", "DSI2", xa2_done)
                p.wait("v", "DSI2", xa2_done)
            vadd = [0, 0]
            for mt in range(8):
                b = 2048 + (mt % 2) * 512
                p.wait("t", "V", vadd[mt % 2])
                for kt in range(8):
                    last = kt == 7
                    p.op("t", lambda t, mt=mt, kt=kt, b=b: t.matmul(
                        ps2[:, b:b + 512],
                        woh[:, kt * 1024 + mt * 128: kt * 1024 + mt * 128 + 128],
                        zh[:, kt * 512:(kt + 1) * 512],
                        start=(kt == 0), stop=(kt == 7)), "T" if last else None)
                tmo = p.cnt["T"]
                p.wait("v", "T", tmo)
                xsl = xa2[:, mt * T + s_i * 512: mt * T + (s_i + 1) * 512]
                vadd[mt % 2] = p.op("v", lambda v, x=xsl, b=b: v.tensor_add(
                    x, x, ps2[:, b:b + 512]), "V")
            p.wait("t", "V", p.cnt["V"])
            if hh == 1:
                emit_ln(xa2, pst=ps2[:, 3072:4096], strips=[s_i])
    stC.close()
    if phases == "post":
        finish_dbg(nc, p, io, xa_f32=xa2)
        stCD.close()
        return
    snapC = p.barrier()

    w8v2 = emit_routing()
    dout = emit_ffn2(2, xa2, w8v2, out_final=True)
    stCD.close()
    p.wait("y", "DOUT", p.cnt["DOUT"])
    p.emit_postamble()

    with nc.Block() as block:
        p.emit(block)


def finish_dbg(nc, p, io, xn_bf=None, xa_f32=None, wait_g=0):
    if xa_f32 is not None:
        p.wait("y", "V", p.cnt["V"])
        p.wait("y", "T", p.cnt["T"])
        p.wait("y", "S", p.cnt["S"])
        p.op("y", lambda y: y.dma_start(
            out=io["outT"].rearrange("(dt q) t -> q dt t", q=128),
            in_=xa_f32[:].rearrange("q (dt t) -> q dt t", dt=NDT)), "DOUT", 16)
    elif xn_bf is not None:
        p.wait("g", "V", p.cnt["V"])
        p.wait("g", "T", p.cnt["T"])
        p.wait("g", "S", p.cnt["S"])
        ncols = xn_bf.shape[1]
        if ncols == NDT * T:
            p.op("g", lambda g: g.dma_start(
                out=io["outT"].rearrange("(dt q) t -> q dt t", q=128),
                in_=xn_bf[:].rearrange("q (dt t) -> q dt t", dt=NDT)), "DOUT", 16)
        else:
            p.op("g", lambda g: g.dma_start(
                out=io["outT"].rearrange("(dt q) t -> q dt t", q=128)[:, 0, 0:ncols],
                in_=xn_bf[:]), "DOUT", 16)
    else:
        p.wait("y", "G", wait_g)
        p.wait("y", "DSP", p.cnt["DSP"])
        p.op("y", lambda y: y.dma_start(
            out=io["outT"].rearrange("(dt q) t -> q dt t", q=128)[:, 0:2, :],
            in_=io["cout"][0:1, :].rearrange("o (q f) -> (o q) f", q=128)[
                :, 0:4096].rearrange("q (dt t) -> q dt t", dt=2)), "DOUT", 16)
    p.wait("y", "DOUT", p.cnt["DOUT"])
    p.emit_postamble()
    with nc.Block() as block:
        p.emit(block)


def _consts(seg):
    import ml_dtypes
    bf = ml_dtypes.bfloat16
    return {
        "ones_col_bf": np.ones((128, 1), bf),
        "ones_row_f32": np.ones((1, 128), np.float32),
        "ones_row_bf": np.ones((1, 128), bf),
        "ones_row512_bf": np.ones((1, 512), bf),
        "eye_bf": np.eye(128, dtype=bf),
        "eye_f32": np.eye(128, dtype=np.float32),
        "triu4_bf": np.ascontiguousarray(np.tile(
            np.triu(np.ones((128, 128), np.float32)) / 16.0, (1, 4)).astype(bf)),
        "iota8": np.ascontiguousarray(np.tile(
            np.arange(8, dtype=np.float32)[None, :], (128, 16))),
        "iotaB": np.ascontiguousarray(np.tile(
            np.arange(8, dtype=np.float32)[None, :] + 1e6, (128, 16))),
        "meta": np.ascontiguousarray(np.broadcast_to(np.array(
            [1.0 if j < seg else 0.0 for j in range(4)],
            np.float32)[None, :], (128, 4))).copy(),
        "cbias": np.ascontiguousarray(np.broadcast_to(np.array(
            [EPS, LNSCALE, LN16, 0.0], np.float32)[None, :], (128, 4))).copy(),
    }


def make_in_maps(inputs):
    x = np.asarray(inputs["L0_attn_x"])
    rmsw = np.asarray(inputs["L0_attn_rms_w"], np.float32)
    shared = {
        "wq": np.asarray(inputs["L0_attn_Wq"], np.float32),
        "wk": np.asarray(inputs["L0_attn_Wk"], np.float32),
        "wv": np.asarray(inputs["L0_attn_Wv"], np.float32),
        "wg": np.asarray(inputs["L0_attn_Wg"], np.float32),
        "wgk1": np.asarray(inputs["L0_attn_Wgk1"], np.float32),
        "wgk2e": np.ascontiguousarray(np.concatenate(
            [np.asarray(inputs["L0_attn_Wgk2"], np.float32),
             np.asarray(inputs["L0_attn_bgk2"], np.float32)[None, :]], axis=0)),
        "wo": np.asarray(inputs["L0_attn_Wo"], np.float32),
        "rmsw4": np.ascontiguousarray(rmsw.reshape(4, 128).T),
        "gatew": np.asarray(inputs["L0_ffn_gate_w"], np.float32),
        "gbias": np.ascontiguousarray(np.broadcast_to(
            np.asarray(inputs["L0_ffn_gate_bias"], np.float32)[None, :],
            (128, 8))).copy(),
        "fwg": np.asarray(inputs["L0_ffn_w_gate"], np.float32),
        "fwu": np.asarray(inputs["L0_ffn_w_up"], np.float32),
        "fwd": np.asarray(inputs["L0_ffn_w_down"], np.float32),
        "swg": np.asarray(inputs["L0_ffn_sw_gate"], np.float32),
        "swu": np.asarray(inputs["L0_ffn_sw_up"], np.float32),
        "swd": np.asarray(inputs["L0_ffn_sw_down"], np.float32),
    }
    maps = []
    for c in range(8):
        b, seg = divmod(c, 4)
        m = dict(shared)
        m["xT"] = np.ascontiguousarray(
            x[b, seg * T:(seg + 1) * T, :].T).astype(np.float32)
        m.update(_consts(seg))
        maps.append(m)
    return maps


def kernel(**inputs):
    nc = build_nc()
    in_maps = make_in_maps(inputs)
    res = run_bass_kernel_spmd(nc, in_maps, core_ids=list(range(8)))
    out = np.empty((2, 8192, 1024), np.float32)
    for c in range(8):
        b, seg = divmod(c, 4)
        out[b, seg * T:(seg + 1) * T, :] = res.results[c]["outT"].T
    return out
